# version 16
# speedup vs baseline: 1.2928x; 1.0142x over previous
import contextlib
import numpy as np
import ml_dtypes
import concourse.bass as bass
import concourse.mybir as mybir
from concourse.bass_utils import run_bass_kernel_spmd

F32 = mybir.dt.float32
BF16 = mybir.dt.bfloat16
AF = mybir.ActivationFunctionType
ALU = mybir.AluOpType
AX = mybir.AxisListType

S, DM = 2048, 2048
NCH = 16
CKV, NKV = 512, 4
CQ, NQC = 1024, 8
NH, DHD, DR = 16, 128, 64
EXP, FF, NFCH = 8, 1408, 11
FS = 5632
NT, TQ = 16, 8
SCALE = float((DHD + DR) ** -0.5)
EPS = 1e-5
QT = [[0, 2, 4, 6, 9, 11, 13, 15], [1, 3, 5, 7, 8, 10, 12, 14]]
PERM = [QT[0] + QT[1], QT[1] + QT[0]]
PADK = []
for half in (0, 1):
    mine, oth = QT[half], QT[1 - half]
    keep = []
    for j in range(TQ):
        n_actual = sum(1 for t in oth if t < mine[j])
        assert n_actual in (j, j + 1)
        keep.append(1.0 if n_actual == j + 1 else 0.0)
    PADK.append(keep)

DEBUG = []
BLOCKS = [0]


class Prog:
    ENGS = ("sp", "pl", "pe", "ac", "ve")

    def __init__(self, nc, es):
        self.nc = nc
        self.es = es
        self.work = {e: [] for e in self.ENGS}
        self.sems = {}
        self.cnt = {}
        for name in ("pe", "ac", "ve"):
            self._mk(name)

    def _mk(self, name):
        if name not in self.sems:
            self.sems[name] = self.es.enter_context(self.nc.semaphore(name))
            self.cnt[name] = 0
        return self.sems[name]

    def val(self, name):
        self._mk(name)
        return self.cnt[name]

    def emit(self, eng, fn, wait=(), inc=()):
        incs = []
        for s, n in inc:
            self._mk(s)
            self.cnt[s] += n
            incs.append((self.sems[s], n))
            if n >= 16:
                if not hasattr(self, "dma_owner"):
                    self.dma_owner = {}
                self.dma_owner[s] = eng
        waits = tuple((self._mk(s), v) for (s, v) in wait if v > 0)
        self.work[eng].append((waits, fn, tuple(incs)))
        return self.cnt

    def chain(self, eng, fns, wait=(), inc=()):
        own = {"pe": "pe", "ac": "ac", "ve": "ve", "pl": "po"}[eng]
        last = tuple(wait)
        for i, fn in enumerate(fns):
            myinc = (((own, 1),) + tuple(inc)) if i == len(fns) - 1 \
                else ((own, 1),)
            self.emit(eng, fn, wait=last, inc=myinc)
            last = ((own, self.val(own)),)
        return self.val(own)

    def final_waits(self):
        for s, eng in getattr(self, "dma_owner", {}).items():
            self.emit(eng, lambda e, sem=self.sems[s], v=self.cnt[s]:
                      e.wait_ge(sem, v))

    def drain_dma(self):
        self.final_waits()
        self.flush()

    def flush(self):
        nc = self.nc
        work, self.work = self.work, {e: [] for e in self.ENGS}
        if not any(work.values()):
            return
        BLOCKS[0] += 1
        with nc.Block() as block:
            def mk(items):
                def body(eng):
                    for waits, fn, incs in items:
                        for sem, v in waits:
                            eng.wait_ge(sem, v)
                        inst = fn(eng)
                        for sem, n in incs:
                            inst.then_inc(sem, n)
                return body
            if work["sp"]:
                block.sync(mk(work["sp"]))
            if work["pl"]:
                block.gpsimd(mk(work["pl"]))
            if work["pe"]:
                block.tensor(mk(work["pe"]))
            if work["ac"]:
                block.scalar(mk(work["ac"]))
            if work["ve"]:
                block.vector(mk(work["ve"]))


def _pk(w):
    w = np.ascontiguousarray(w)
    R, C = w.shape
    return np.ascontiguousarray(
        w.reshape(R // 128, 128, C).transpose(1, 0, 2).reshape(128, -1))


def build_program():
    nc = bass.Bass("TRN2", target_bir_lowering=False)
    es0 = contextlib.ExitStack()
    taps = {}

    def dram(name, shape, dt=F32, kind="ExternalInput"):
        return nc.dram_tensor(name, shape, dt, kind=kind).ap()

    x_in = dram("x_in", [NT, 128, DM])
    wdkv = dram("wdkv", [128, NCH * CKV])
    wdq = dram("wdq", [128, NCH * CQ])
    wkr = dram("wkr", [128, NCH * DR])
    wuk = dram("wuk", [128, NKV * DM])
    wuv = dram("wuv", [128, NKV * DM])
    wuq = dram("wuq", [128, NQC * DM])
    wqr = dram("wqr", [128, NQC * CQ])
    wout = dram("wout", [128, NCH * DM])
    wr_i = dram("wr_i", [128, NCH * EXP])
    wsg_i = dram("wsg_i", [128, NCH])
    wg_i = dram("wg_i", [EXP, NFCH, 128, DM])
    wu_i = dram("wu_i", [EXP, NFCH, 128, DM])
    sg_i = dram("sg_i", [4 * NFCH, 128, DM])
    su_i = dram("su_i", [4 * NFCH, 128, DM])
    wd_i = dram("wd_i", [EXP, FF, DM])
    sd_i = dram("sd_i", [FS, DM])
    cosA = dram("cosA", [32, S])
    sinA = dram("sinA", [32, S])
    delta_i = dram("delta_i", [32, 1])
    triT = dram("triT", [128, 128])
    padk_i = dram("padk_i", [128, TQ])
    ones_f = dram("ones_f", [1, 128])
    onescol_f = dram("onescol_f", [128, 1])
    y_out = dram("y", [TQ, 128, DM], kind="ExternalOutput")
    xprime = dram("xprime", [TQ, 128, DM],
                  kind="ExternalOutput" if "xprime" in DEBUG else "Internal")

    wd_r = wd_i.rearrange("e (f p) o -> e p f o", p=128)
    sd_r = sd_i.rearrange("(f p) o -> p f o", p=128)

    def tap(name, shape, dt=F32):
        if name in DEBUG:
            taps[name] = dram(name + "_tap", shape, dt, kind="ExternalOutput")
            return taps[name]
        return None

    with es0:
        p = Prog(nc, es0)

        def sb(name, shape, dt=F32):
            return es0.enter_context(nc.sbuf_tensor(name, shape, dt))

        def do_tap(name, src_ap):
            if name not in DEBUG:
                return
            shape = list(src_ap.shape)
            t = dram(name + "_tap", shape, src_ap.dtype, kind="ExternalOutput")
            taps[name] = t
            p.emit("sp", lambda e, t=t, s=src_ap: e.dma_start(t[:, :], s),
                   wait=(), inc=(("stp", 16),))
            p.emit("sp", lambda e, v=p.val("stp"): e.wait_ge(p.sems["stp"], v))

        c_identb = sb("c_identb", [128, 128], BF16)
        c_onesb = sb("c_onesb", [128, 128], BF16)
        c_onesf = sb("c_onesf", [1, 128])
        c_onescf = sb("c_onescf", [128, 1])
        c_tri = sb("c_tri", [128, 128], BF16)
        c_padk = sb("c_padk", [128, TQ])

        es_attn = contextlib.ExitStack()
        with es_attn:
            def sba(name, shape, dt=F32):
                return es_attn.enter_context(nc.sbuf_tensor(name, shape, dt))

            ckv_fm = sba("ckv_fm", [128, NKV * S], BF16)
            cq_fm = sba("cq_fm", [128, NQC * 1024], BF16)
            krot = sba("krot", [64, S], BF16)
            krot_b = sba("krot_b", [64, 1])
            cosQ = sba("cosQ", [32, 1024], BF16)
            sinQ = sba("sinQ", [32, 1024], BF16)

            es_ph0 = contextlib.ExitStack()
            with es_ph0:
                def sb0(name, shape, dt=F32):
                    return es_ph0.enter_context(nc.sbuf_tensor(name, shape, dt))
                junk = sb0("junk", [128, DM], BF16)
                w_dkv = sb0("w_dkv", [128, NCH * CKV], BF16)
                w_dq = sb0("w_dq", [128, NCH * CQ], BF16)
                w_kr = sb0("w_kr", [128, NCH * DR], BF16)
                t_delta = sb0("t_delta", [32, 1])
                kr_raw = sb0("kr_raw", [64, S])
                cosT = sb0("cosT", [32, S], BF16)
                sinT = sb0("sinT", [32, S], BF16)
                tmp_ra = sb0("tmp_ra", [32, S])
                tmp_rb = sb0("tmp_rb", [32, S])
                t_cosA = tmp_ra
                t_sinA = tmp_rb
                tmpT = kr_raw[0:32, :]

                for dst, src in ((c_onesf, ones_f),
                                 (c_onescf, onescol_f), (t_cosA, cosA),
                                 (t_sinA, sinA), (t_delta, delta_i)):
                    p.emit("sp", lambda e, d=dst, s=src: e.dma_start(d[:, :], s[:, :]),
                           inc=(("scst", 16),))
                v_consts = p.val("scst")
                p.emit("pl", lambda e: e.dma_start(c_tri[:, :], triT[:, :]),
                       inc=(("pcst", 16),))
                p.emit("sp", lambda e: e.dma_start(c_padk[:, :], padk_i[:, :]),
                       inc=(("spk", 16),))
                v_padk = p.val("spk")
                v_consts2 = p.val("pcst")
                p.emit("pl", lambda e, v=v_consts2: e.wait_ge(p.sems["pcst"], v))
                p.emit("pl", lambda e: e.iota(
                    junk[:, 0:128], [[1, 128]], channel_multiplier=0,
                    allow_small_or_imprecise_dtypes=True), inc=(("po", 1),))
                p.emit("pl", lambda e: e.iota(
                    iotc[:, :], [[0, 1]], channel_multiplier=1,
                    allow_small_or_imprecise_dtypes=True), inc=(("po", 1),))
                v_iot = p.val("po")
                p.emit("ve", lambda e: e.tensor_scalar(
                    c_identb[:, :], junk[:, 0:128], iotc[:, :], None,
                    ALU.is_equal), wait=(("po", v_iot),), inc=(("ve", 1),))
                v_identb = p.val("ve")
                p.emit("ve", lambda e: e.memset(c_onesb[:, :], 1.0))
                for dst, src in ((w_dkv, wdkv), (w_dq, wdq), (w_kr, wkr)):
                    p.emit("pl", lambda e, d=dst, s=src: e.dma_start(d[:, :], s[:, :]),
                           inc=(("pw0", 16),))
                v_phaw = p.val("pw0")

                d_red = sb0("d_red", [32, 2])
                iotc = sb0("iotc", [128, 1])
                dm_sh = sb0("dm_sh", [32, 2])
                TWO_PI = float(2 * np.pi)
                steps = [
                    lambda e: e.tensor_scalar(d_red[:, 0:1], t_delta[:, :],
                                              0.0, None, ALU.add),
                    lambda e: e.tensor_scalar(d_red[:, 1:2], t_delta[:, :],
                                              float(np.pi / 2), None, ALU.add),
                ]
                for _ in range(3):
                    steps += [
                        lambda e: e.tensor_scalar(dm_sh[:, :], d_red[:, :],
                                                  float(np.pi), None, ALU.is_ge),
                        lambda e: e.scalar_tensor_tensor(
                            d_red[:, :], dm_sh[:, :], -TWO_PI, d_red[:, :],
                            ALU.mult, ALU.add),
                        lambda e: e.tensor_scalar(dm_sh[:, :], d_red[:, :],
                                                  float(-np.pi), None, ALU.is_lt),
                        lambda e: e.scalar_tensor_tensor(
                            d_red[:, :], dm_sh[:, :], TWO_PI, d_red[:, :],
                            ALU.mult, ALU.add),
                    ]
                v_dred = p.chain("ve", steps, wait=(("scst", v_consts),))
                d_sc = sb0("d_sc", [32, 2])
                p.emit("ac", lambda e: e.activation(d_sc[:, :], d_red[:, :], AF.Sin),
                       wait=(("ve", v_dred),), inc=(("ac", 1),))
                v_dsc = p.val("ac")
                v_tables = p.chain("ve", [
                    lambda e: e.tensor_scalar(cosT[:, :], t_cosA[:, :],
                                              d_sc[:, 1:2], None, ALU.mult),
                    lambda e: e.tensor_scalar(tmpT[:, :], t_sinA[:, :],
                                              d_sc[:, 0:1], None, ALU.mult),
                    lambda e: e.tensor_tensor(cosT[:, :], cosT[:, :],
                                              tmpT[0:32, :], ALU.subtract),
                    lambda e: e.tensor_scalar(sinT[:, :], t_sinA[:, :],
                                              d_sc[:, 1:2], None, ALU.mult),
                    lambda e: e.tensor_scalar(tmpT[0:32, :], t_cosA[:, :],
                                              d_sc[:, 0:1], None, ALU.mult),
                    lambda e: e.tensor_tensor(sinT[:, :], sinT[:, :],
                                              tmpT[0:32, :], ALU.add),
                    lambda e: e.tensor_copy(cosQ[:, :], cosT[:, 0:1024]),
                    lambda e: e.tensor_copy(sinQ[:, :], sinT[:, 0:1024]),
                ], wait=(("ac", v_dsc),))
                p.flush()
                do_tap("t_delta", t_delta[:, :])
                do_tap("t_cosA", t_cosA[0:32, 0:8])
                do_tap("d_red", d_red[:, :])
                do_tap("d_sc", d_sc[:, :])
                do_tap("cosT", cosT[:, :])
                do_tap("sinT", sinT[:, :])

                xbuf = [sb0(f"xbuf{i}", [128, DM]) for i in range(2)]
                xln_tm = [sb0(f"xln{i}", [128, DM], BF16) for i in range(2)]
                xfm = sb0("xfm", [128, NCH * 512], BF16)
                st_sum = sb0("st_sum", [128, 2])
                st_sq = sb0("st_sq", [128, 2])
                st_mv = sb0("st_mv", [128, 8])
                es_psA = contextlib.ExitStack()
                with es_psA:
                    psT = [es_psA.enter_context(nc.psum_tensor(f"psT{i}", [128, 1024], BF16))
                           for i in range(2)]
                    psM = [es_psA.enter_context(nc.psum_tensor(f"psM{i}", [128, 512], F32))
                           for i in range(2)]
                    psK = es_psA.enter_context(nc.psum_tensor("psK", [64, 512], F32))

                    v_xdma = {}
                    v_norm = {}
                    v_evac = {}
                    v_mmg = {}
                    v_trlast = {}
                    v_last_ev = {}
                    v_krev = {0: 0}
                    nmm = 0
                    def emit_xload(t):
                        w = []
                        if t >= 2:
                            w.append(("ve", v_norm[t - 2]))
                        sx = f"sx{t % 2}"
                        p.emit("sp", lambda e, b=t % 2, t=t: e.dma_start(
                            xbuf[b][:, :], x_in[t]), wait=tuple(w),
                            inc=((sx, 16),))
                        v_xdma[t] = (sx, p.val(sx))
                    emit_xload(0)
                    emit_xload(1)
                    v_mv = {}
                    for g in range(4):
                        for ti in range(4):
                            t = 4 * g + ti
                            bi = t % 2
                            wx = v_xdma[t]
                            wst = [wx]
                            if t >= 2:
                                wst.append(("ve", v_mv[t - 2]))
                            p.emit("ve", lambda e, b=bi: e.reduce_sum(
                                st_sum[:, b:b + 1], xbuf[b][:, :], axis=AX.X),
                                wait=tuple(wst), inc=(("ve", 1),))
                            v_rsum = p.val("ve")
                            p.emit("ac", lambda e, b=bi: e.activation(
                                junk[:, :], xbuf[b][:, :], AF.Square,
                                accum_out=st_sq[:, b:b + 1]),
                                wait=(wx, ("scst", v_consts), ("ve", v_identb)),
                                inc=(("ac", 1),))
                            v_sq = p.val("ac")
                            o = 4 * bi
                            wmv = [("ac", v_sq), ("ve", v_rsum)]
                            if t >= 2:
                                wmv.append(("ve", v_norm[t - 2]))
                            v_mv[t] = p.chain("ve", [
                                lambda e, b=bi, o=o: e.tensor_scalar(
                                    st_mv[:, o:o + 1], st_sum[:, b:b + 1],
                                    1.0 / DM, None, ALU.mult),
                                lambda e, b=bi, o=o: e.tensor_scalar(
                                    st_mv[:, o + 1:o + 2], st_sq[:, b:b + 1],
                                    1.0 / DM, EPS, ALU.mult, ALU.add),
                                lambda e, o=o: e.tensor_tensor(
                                    st_mv[:, o + 2:o + 3], st_mv[:, o:o + 1],
                                    st_mv[:, o:o + 1], ALU.mult),
                                lambda e, o=o: e.tensor_tensor(
                                    st_mv[:, o + 1:o + 2], st_mv[:, o + 1:o + 2],
                                    st_mv[:, o + 2:o + 3], ALU.subtract),
                            ], wait=tuple(wmv))
                            v_rstd = p.chain("ac", [
                                lambda e, o=o: e.activation(
                                    st_mv[:, o + 2:o + 3], st_mv[:, o + 1:o + 2],
                                    AF.Ln),
                                lambda e, o=o: e.activation(
                                    st_mv[:, o + 3:o + 4], st_mv[:, o + 2:o + 3],
                                    AF.Exp, scale=-0.5),
                            ], wait=(("ve", v_mv[t]),))
                            wnorm = [("ac", v_rstd)]
                            if t >= 2:
                                wnorm.append(("pe", v_trlast[t - 2]))
                            v_norm[t] = p.chain("ve", [
                                lambda e, o=o: e.scalar_tensor_tensor(
                                    st_mv[:, o:o + 1], st_mv[:, o:o + 1], -1.0,
                                    st_mv[:, o + 3:o + 4], ALU.mult, ALU.mult),
                                lambda e, b=bi, o=o: e.tensor_scalar(
                                    xln_tm[b][:, :], xbuf[b][:, :],
                                    st_mv[:, o + 3:o + 4], st_mv[:, o:o + 1],
                                    ALU.mult, ALU.add),
                            ], wait=tuple(wnorm))
                            if t + 2 < NT:
                                emit_xload(t + 2)
                            for hb in range(2):
                                wtr = [("ve", v_norm[t])]
                                if t >= 1:
                                    wtr.append(("ve", v_evac.get((t - 1, hb), 0)))
                                if g > 0:
                                    wtr.append(("pe", v_mmg.get(g - 1, 0)))
                                def _tr(e, b=bi, hb=hb):
                                    for c8 in range(8):
                                        c = hb * 8 + c8
                                        inst = e.transpose(
                                            psT[hb][:, c8 * 128:(c8 + 1) * 128],
                                            xln_tm[b][:, c * 128:(c + 1) * 128],
                                            c_identb[:, :])
                                    return inst
                                p.emit("pe", _tr, wait=tuple(wtr), inc=(("pe", 1),))
                                v_tr = p.val("pe")
                                def _ev(e, hb=hb, ti=ti):
                                    dst = xfm[:, :].rearrange(
                                        "p (c q) -> p c q", c=NCH)[
                                        :, hb * 8:(hb + 1) * 8,
                                        ti * 128:(ti + 1) * 128]
                                    return e.tensor_copy(
                                        dst, psT[hb][:, :].rearrange(
                                            "p (c q) -> p c q", c=8))
                                p.emit("ve", _ev, wait=(("pe", v_tr),),
                                       inc=(("ve", 1),))
                                v_evac[(t, hb)] = p.val("ve")
                                v_trlast[t] = v_tr
                        wg_ready = [("ve", v_evac[(4 * g + 3, 1)]),
                                    ("pw0", v_phaw)]
                        xfm_r = xfm[:, :].rearrange("p (c q) -> p c q", c=NCH)
                        outs = [("ckv", ck) for ck in range(NKV)]
                        if g < 2:
                            outs += [("cq", qc) for qc in range(NQC)]
                        outs += [("kr", 0)]
                        for kind, ck in outs:
                            pb = nmm % 2
                            wmm = list(wg_ready)
                            if nmm >= 2:
                                wmm.append(("ac", v_last_ev[nmm - 2]))
                            if kind == "ckv":
                                def _mm(e, ck=ck, pb=pb):
                                    for c in range(NCH):
                                        inst = e.matmul(
                                            psM[pb][:, :],
                                            w_dkv[:, c * CKV + ck * 128:
                                                  c * CKV + ck * 128 + 128],
                                            xfm_r[:, c, :],
                                            start=(c == 0), stop=(c == NCH - 1))
                                    return inst
                                dst = ckv_fm[:, :].rearrange(
                                    "p (k s) -> p k s", k=NKV)[:, ck,
                                    g * 512:(g + 1) * 512]
                                psrc = psM[pb]
                            elif kind == "cq":
                                def _mm(e, ck=ck, pb=pb):
                                    for c in range(NCH):
                                        inst = e.matmul(
                                            psM[pb][:, :],
                                            w_dq[:, c * CQ + ck * 128:
                                                 c * CQ + ck * 128 + 128],
                                            xfm_r[:, c, :],
                                            start=(c == 0), stop=(c == NCH - 1))
                                    return inst
                                dst = cq_fm[:, :].rearrange(
                                    "p (k s) -> p k s", k=NQC)[:, ck,
                                    g * 512:(g + 1) * 512]
                                psrc = psM[pb]
                            else:
                                def _mm(e, pb=pb):
                                    for c in range(NCH):
                                        inst = e.matmul(
                                            psK[:, :],
                                            w_kr[:, c * DR:(c + 1) * DR],
                                            xfm_r[:, c, :],
                                            start=(c == 0), stop=(c == NCH - 1))
                                    return inst
                                dst = kr_raw[:, g * 512:(g + 1) * 512]
                                psrc = psK
                            if kind == "kr":
                                wmm.append(("ve", v_krev[g]))
                            p.emit("pe", _mm, wait=tuple(wmm), inc=(("pe", 1),))
                            v_mm = p.val("pe")
                            if kind == "kr":
                                p.emit("ve", lambda e, d=dst, s=psrc: e.tensor_copy(
                                    d, s[:, :]),
                                    wait=(("pe", v_mm), ("ve", v_tables)),
                                    inc=(("ve", 1),))
                                v_krev[g + 1] = p.val("ve")
                            else:
                                p.emit("ac", lambda e, d=dst, s=psrc: e.copy(
                                    d, s[:, :]), wait=(("pe", v_mm),),
                                    inc=(("ac", 1),))
                                v_last_ev[nmm] = p.val("ac")
                                nmm += 1
                        v_mmg[g] = p.val("pe")
                    p.flush()
                mu_k1 = sb0("mu_k1", [32, S])
                mu_k2 = sb0("mu_k2", [32, S])
                v_muk = p.chain("ac", [
                    lambda e: e.activation(mu_k1[:, :], kr_raw[0:32, :], AF.Exp),
                    lambda e: e.activation(mu_k1[:, :], mu_k1[:, :], AF.Ln,
                                           bias=1.0),
                    lambda e: e.activation(mu_k2[:, :], kr_raw[32:64, :], AF.Exp),
                    lambda e: e.activation(mu_k2[:, :], mu_k2[:, :], AF.Ln,
                                           bias=1.0),
                ], wait=(("ve", p.val("ve")),))
                def emit_rope(mu1, mu2, cosS, sinS, out, n, waits):
                    return p.chain("ve", [
                        lambda e: e.tensor_tensor(
                            tmp_ra[:, 0:n], mu1[:, 0:n], cosS[:, 0:n], ALU.mult),
                        lambda e: e.tensor_tensor(
                            tmp_rb[:, 0:n], mu2[:, 0:n], sinS[:, 0:n], ALU.mult),
                        lambda e: e.tensor_tensor(
                            out[0:32, 0:n], tmp_ra[:, 0:n], tmp_rb[:, 0:n],
                            ALU.subtract),
                        lambda e: e.tensor_tensor(
                            tmp_ra[:, 0:n], mu1[:, 0:n], sinS[:, 0:n], ALU.mult),
                        lambda e: e.tensor_tensor(
                            tmp_rb[:, 0:n], mu2[:, 0:n], cosS[:, 0:n], ALU.mult),
                        lambda e: e.tensor_tensor(
                            out[32:64, 0:n], tmp_ra[:, 0:n], tmp_rb[:, 0:n],
                            ALU.add),
                    ], wait=tuple(waits))
                emit_rope(mu_k1, mu_k2, cosT, sinT, krot, S,
                          (("ac", v_muk), ("ve", v_tables)))
                p.chain("ve", wait=(("ve", p.val("ve")),), fns=[
                    lambda e: e.reduce_max(krot_b[:, 0:1], krot[:, :],
                                           axis=AX.X,
                                           apply_absolute_value=True),
                    lambda e: e.tensor_tensor(krot_b[:, 0:1], krot_b[:, 0:1],
                                              krot_b[:, 0:1], ALU.mult),
                ])
                p.flush()
                do_tap("ckv", ckv_fm[:, :], )
                do_tap("cq", cq_fm[:, :])
                do_tap("krot", krot[:, :])
                p.drain_dma()

            att_all = sba("att_all", [128, 8 * TQ * 128], BF16)
            wout_sb = sba("wout_sb", [128, 8 * DM], BF16)
            xc = [sba(f"xc{i}", [128, DM]) for i in range(2)]
            es_phB = contextlib.ExitStack()
            with es_phB:
                def sbb(name, shape, dt=F32):
                    return es_phB.enter_context(nc.sbuf_tensor(name, shape, dt))
                uk_h = [sbb(f"uk_h{i}", [128, NKV * 128], BF16) for i in range(2)]
                uv_g = [sbb(f"uv_g{i}", [128, NKV * 512], BF16) for i in range(2)]
                K_ch2 = [sbb(f"K_ch{i}", [128, S], BF16) for i in range(2)]
                V_g2 = [sbb(f"V_g{i}", [128, 16 * 512], BF16) for i in range(2)]
                q_c2 = [sbb(f"q_c{i}", [128, 1024], BF16) for i in range(2)]
                mu_q1_2 = [sbb(f"mu_q1_{i}", [32, 1024], BF16) for i in range(2)]
                mu_q2_2 = [sbb(f"mu_q2_{i}", [32, 1024], BF16) for i in range(2)]
                q_rot2 = [sbb(f"q_rot{i}", [64, 1024], BF16) for i in range(2)]
                tmp_qa2 = [sbb(f"tmp_qa{i}", [32, 1024], BF16) for i in range(2)]
                tmp_qb2 = [sbb(f"tmp_qb{i}", [32, 1024], BF16) for i in range(2)]
                exp_T = [sbb(f"exp_T{i}", [128, S], BF16) for i in range(2)]
                uq_h = [sbb(f"uq_h{i}", [128, NQC * 128], BF16) for i in range(2)]
                qr_h = [sbb(f"qr_h{i}", [128, NQC * 64], BF16) for i in range(2)]
                sc_e_2 = [sbb(f"sc_e_{i}", [128, 128], BF16) for i in range(2)]
                sc_e2_2 = [sbb(f"sc_e2_{i}", [128, 128], BF16) for i in range(2)]
                sc_rcp = sbb("sc_rcp", [128, 128])
                st_b2 = [sbb(f"st_b{i}", [128, 5]) for i in range(2)]
                sb_n = sbb("sb_n", [1, 4])
                negb_bc2 = [sbb(f"negb_bc{i}", [128, 1]) for i in range(2)]
                ckv_r = ckv_fm[:, :].rearrange("p (c s) -> p c s", c=NKV)
                cq_r = cq_fm[:, :].rearrange("p (c s) -> p c s", c=NQC)
                V_r2 = [t[:, :].rearrange("p (t2 s) -> p t2 s", t2=16)
                        for t in V_g2]
                wout_r = wout_sb[:, :].rearrange("p (c s) -> p c s", c=8)

                es_psB = contextlib.ExitStack()
                with es_psB:
                    psS = [es_psB.enter_context(
                        nc.psum_tensor(f"psS{i}", [128, 512], F32)) for i in range(2)]
                    psB = [es_psB.enter_context(
                        nc.psum_tensor(f"psB{i}", [128, 512], F32)) for i in range(2)]
                    psQ = [es_psB.enter_context(
                        nc.psum_tensor(f"psQ{i}", [128, 512], F32)) for i in range(2)]
                    psSum = es_psB.enter_context(
                        nc.psum_tensor("psSum", [128, 128], F32))[:, :]
                    psAtt = es_psB.enter_context(
                        nc.psum_tensor("psAtt", [128, 128], F32))[:, :]
                    psBB = psQ[0][:, 256:257]
                    psN = psQ[0][0:1, 384:386]

                    def emit_uq_load(h, wprev=()):
                        sl = h % 2
                        sem = f"puq{sl}"
                        p.emit("pl", lambda e, h=h, sl=sl: e.dma_start(
                            uq_h[sl][:, :].rearrange("p (c s) -> p c s", c=NQC),
                            wuq.rearrange("p (c s) -> p c s", c=NQC)[
                                :, :, h * 128:(h + 1) * 128]),
                            wait=tuple(wprev), inc=((sem, 16),))
                        p.emit("pl", lambda e, h=h, sl=sl: e.dma_start(
                            qr_h[sl][:, :].rearrange("p (c s) -> p c s", c=NQC),
                            wqr.rearrange("p (c s) -> p c s", c=NQC)[
                                :, :, h * 64:(h + 1) * 64]),
                            inc=((sem, 16),))
                        p.emit("pl", lambda e, h=h, sl=sl: e.dma_start(
                            uk_h[sl][:, :].rearrange("p (c s) -> p c s", c=NKV),
                            wuk.rearrange("p (c s) -> p c s", c=NKV)[
                                :, :, h * 128:(h + 1) * 128]),
                            inc=((sem, 16),))
                        return (sem, p.val(sem))

                    v_vmm_grp = {-2: 0, -1: 0}
                    v_psB_last = [0]

                    def emit_uv_load(g4p):
                        sl = g4p % 2
                        sem = f"puv{sl}"
                        p.emit("pl", lambda e, g4p=g4p, sl=sl: e.dma_start(
                            uv_g[sl][:, :].rearrange("p (c s) -> p c s", c=NKV),
                            wuv.rearrange("p (c s) -> p c s", c=NKV)[
                                :, :, g4p * 512:(g4p + 1) * 512]),
                            wait=(("pe", v_vmm_grp[g4p - 2]),),
                            inc=((sem, 16),))
                        return (sem, p.val(sem))

                    state = {"v_phc_last": 0, "v_attev_half": 0,
                             "ssp_tot": {0: 0, 1: 0}}

                    def emit_phc(k, v_wout_k, v_att_ready):
                        v_ld = {}
                        v_st = {0: 0, 1: 0}
                        v_mmD = {}
                        v_evD = {}
                        last_mm = 0
                        last_ev = 0
                        def phc_load(j, k):
                            sl = j % 2
                            sem = f"sxc{sl}"
                            w = []
                            if j >= 2:
                                w.append((f"ssp{sl}", v_st[sl]))
                            if k == 1 and j < 2:
                                w.append(("ssp0", state["ssp_tot"][0]))
                                w.append(("ssp1", state["ssp_tot"][1]))
                            srcx = x_in[j] if k == 0 else xprime[j]
                            p.emit("sp", lambda e, s=srcx, sl=sl: e.dma_start(
                                xc[sl][:, :], s), wait=tuple(w),
                                inc=((sem, 16),))
                            v_ld[j] = (sem, p.val(sem))
                        for j0 in range(2):
                            phc_load(j0, k)
                        for j in range(TQ):
                            sl = j % 2
                            for ob in range(4):
                                pb = ob % 2
                                def _omm(e, j=j, ob=ob, pb=pb, k=k):
                                    for hh in range(8):
                                        inst = e.matmul(
                                            psS[pb][:, :],
                                            att_all[:, (hh * 8 + j) * 128:
                                                    (hh * 8 + j + 1) * 128],
                                            wout_r[:, hh, ob * 512:(ob + 1) * 512],
                                            start=(hh == 0), stop=(hh == 7))
                                    return inst
                                wmm = [v_wout_k, ("ve", v_att_ready)]
                                key = (j * 4 + ob) - 2
                                if key >= 0:
                                    wmm.append(("ve", v_evD[key]))
                                p.emit("pe", _omm, wait=tuple(wmm),
                                       inc=(("pe", 1),))
                                last_mm = p.val("pe")
                                v_mmD[j * 4 + ob] = last_mm
                                p.emit("ve", lambda e, j=j, ob=ob, pb=pb, sl=sl:
                                       e.tensor_tensor(
                                           xc[sl][:, ob * 512:(ob + 1) * 512],
                                           psS[pb][:, :],
                                           xc[sl][:, ob * 512:(ob + 1) * 512],
                                           ALU.add),
                                       wait=(("pe", last_mm), v_ld[j]),
                                       inc=(("ve", 1),))
                                last_ev = p.val("ve")
                                v_evD[j * 4 + ob] = last_ev
                            p.emit("sp", lambda e, j=j, sl=sl: e.dma_start(
                                xprime[j], xc[sl][:, :]),
                                wait=(("ve", last_ev),), inc=((f"ssp{sl}", 16),))
                            v_st[sl] = p.val(f"ssp{sl}")
                            if j + 2 < TQ:
                                phc_load(j + 2, k)
                        for sli in range(2):
                            state["ssp_tot"][sli] = p.val(f"ssp{sli}")
                        return last_mm, last_ev

                    v_uq = {}
                    v_qbuild = {-2: 0, -1: 0}
                    v_qrexp = {-1: 0}
                    v_qcev = {}
                    v_rope = {}
                    v_vb_ev = {}
                    v_kb_ev = {}
                    v_expS = {}
                    v_wout = {}
                    v_negb = {}
                    v_acend = {-2: 0, -1: 0}
                    v_bn_pe = {-2: 0, -1: 0}
                    v_qbound_ve = {-2: 0, -1: 0}
                    bn_pending = {}
                    v_bnev_last = [0]

                    def flush_pend():
                        if state.get("pend") is not None:
                            emit_tail(*state["pend"])
                            state["pend"] = None

                    def emit_prelude(hp):
                        par = hp % 2
                        q_c = q_c2[par]
                        if 2 <= hp + 1 < NH:
                            v_uq[hp + 1] = emit_uq_load(
                                hp + 1, wprev=(("pe", v_qbuild[hp - 1]),))
                        for tb in range(2):
                            pb = tb % 2
                            def _qmm(e, tb=tb, hp=hp):
                                for c in range(NQC):
                                    inst = e.matmul(
                                        psQ[tb][:, :],
                                        uq_h[hp % 2][:, c * 128:(c + 1) * 128],
                                        cq_r[:, c, tb * 512:(tb + 1) * 512],
                                        start=(c == 0), stop=(c == NQC - 1))
                                return inst
                            p.emit("pe", _qmm,
                                   wait=(v_uq[hp], ("ac", v_qrexp[hp - 1]),
                                         ("ve", v_negb.get(hp - 1, 0))),
                                   inc=(("pe", 1),))
                            v_m = p.val("pe")
                            p.emit("ac", lambda e, tb=tb, q_c=q_c: e.copy(
                                q_c[:, tb * 512:(tb + 1) * 512], psQ[tb][:, :]),
                                wait=(("pe", v_m), ("ve", v_qbound_ve[hp - 2])),
                                inc=(("ac", 1),))
                        v_qcev[hp] = p.val("ac")

                    def emit_prelude_qr(hp):
                        par = hp % 2
                        q_rot = q_rot2[par]
                        mu_q1 = mu_q1_2[par]
                        mu_q2 = mu_q2_2[par]
                        tmp_qa = tmp_qa2[par]
                        tmp_qb = tmp_qb2[par]
                        for tb in range(2):
                            pb = tb % 2
                            def _qrm(e, tb=tb, hp=hp):
                                for c in range(NQC):
                                    inst = e.matmul(
                                        psQ[tb][0:64, :],
                                        qr_h[hp % 2][:, c * 64:(c + 1) * 64],
                                        cq_r[:, c, tb * 512:(tb + 1) * 512],
                                        start=(c == 0), stop=(c == NQC - 1))
                                return inst
                            p.emit("pe", _qrm,
                                   wait=(("ac", v_qcev[hp]), v_uq[hp]),
                                   inc=(("pe", 1),))
                            v_m = p.val("pe")
                            p.chain("ac", [
                                lambda e, tb=tb, mu_q1=mu_q1: e.activation(
                                    mu_q1[:, tb * 512:(tb + 1) * 512],
                                    psQ[tb][0:32, :], AF.Exp),
                                lambda e, tb=tb, mu_q2=mu_q2: e.activation(
                                    mu_q2[:, tb * 512:(tb + 1) * 512],
                                    psQ[tb][32:64, :], AF.Exp),
                            ], wait=(("pe", v_m),
                                     ("po", v_rope.get(hp - 2, 0)),
                                     ("ve", v_qbound_ve[hp - 2])))
                        v_qrexp[hp] = p.val("ac")
                        v_spq = p.chain("ac", [
                            lambda e, mu_q1=mu_q1: e.activation(
                                mu_q1[:, :], mu_q1[:, :], AF.Ln, bias=1.0),
                            lambda e, mu_q2=mu_q2: e.activation(
                                mu_q2[:, :], mu_q2[:, :], AF.Ln, bias=1.0),
                        ])
                        state["v_spq"][hp] = v_spq
                        def rope_ops(lo, hi):
                            sl = slice(lo, hi)
                            return [
                                lambda e, a=tmp_qa, m=mu_q1: e.tensor_tensor(
                                    a[:, sl], m[:, sl], cosQ[:, sl], ALU.mult),
                                lambda e, b=tmp_qb, m=mu_q2: e.tensor_tensor(
                                    b[:, sl], m[:, sl], sinQ[:, sl], ALU.mult),
                                lambda e, q_rot=q_rot, a=tmp_qa, b=tmp_qb:
                                    e.tensor_tensor(q_rot[0:32, sl], a[:, sl],
                                                    b[:, sl], ALU.subtract),
                                lambda e, a=tmp_qa, m=mu_q1: e.tensor_tensor(
                                    a[:, sl], m[:, sl], sinQ[:, sl], ALU.mult),
                                lambda e, b=tmp_qb, m=mu_q2: e.tensor_tensor(
                                    b[:, sl], m[:, sl], cosQ[:, sl], ALU.mult),
                                lambda e, q_rot=q_rot, a=tmp_qa, b=tmp_qb:
                                    e.tensor_tensor(q_rot[32:64, sl], a[:, sl],
                                                    b[:, sl], ALU.add),
                            ]
                        v_rope[hp] = p.chain("pl", rope_ops(0, 1024),
                                             wait=(("ac", v_spq),
                                                   ("pe", v_bn_pe[hp - 2])))

                    def emit_prelude_kv(hp):
                        par = hp % 2
                        g4p = hp // 4
                        K_ch = K_ch2[par]
                        q_c = q_c2[par]
                        q_rot = q_rot2[par]
                        st_b = st_b2[par]
                        for tb in range(4):
                            pb = tb % 2
                            def _kmm(e, tb=tb, pb=pb, hp=hp):
                                for c in range(NKV):
                                    inst = e.matmul(
                                        psB[pb][:, :],
                                        uk_h[hp % 2][:, c * 128:(c + 1) * 128],
                                        ckv_r[:, c, tb * 512:(tb + 1) * 512],
                                        start=(c == 0), stop=(c == NKV - 1))
                                return inst
                            wmm = [v_uq[hp],
                                   ("ac", v_psB_last[0])] if tb < 2 else \
                                [v_uq[hp], ("ac", v_kb_ev[(hp, tb - 2)])]
                            p.emit("pe", _kmm, wait=tuple(wmm), inc=(("pe", 1),))
                            v_m = p.val("pe")
                            p.emit("ac", lambda e, tb=tb, pb=pb, K_ch=K_ch: e.copy(
                                K_ch[:, tb * 512:(tb + 1) * 512], psB[pb][:, :]),
                                wait=(("pe", v_m),), inc=(("ac", 1),))
                            v_kb_ev[(hp, tb)] = p.val("ac")
                        v_qbuild[hp] = v_m
                        if hp % 4 != 0:
                            v_psB_last[0] = v_kb_ev[(hp, 3)]
                        if hp % 4 == 0:
                            Vg = V_g2[g4p % 2]
                            v_uvl = state["v_uvl"][g4p]
                            uvg = uv_g[g4p % 2]
                            for kt in range(16):
                                pb = kt % 2
                                def _vmm(e, kt=kt, pb=pb, uvg=uvg):
                                    for c in range(NKV):
                                        inst = e.matmul(
                                            psB[pb][:, :],
                                            ckv_r[:, c, kt * 128:(kt + 1) * 128],
                                            uvg[:, c * 512:(c + 1) * 512],
                                            start=(c == 0), stop=(c == NKV - 1))
                                    return inst
                                wmm = [v_uvl,
                                       ("ac", v_kb_ev[(hp, 2 + pb)])] if kt < 2 else \
                                    [v_uvl, ("ac", v_vb_ev[(g4p, kt - 2)])]
                                p.emit("pe", _vmm, wait=tuple(wmm),
                                       inc=(("pe", 1),))
                                v_m = p.val("pe")
                                p.emit("ac", lambda e, kt=kt, pb=pb, Vg=Vg: e.copy(
                                    Vg[:, kt * 512:(kt + 1) * 512], psB[pb][:, :]),
                                    wait=(("pe", v_m),), inc=(("ac", 1),))
                                v_vb_ev[(g4p, kt)] = p.val("ac")
                            v_psB_last[0] = v_vb_ev[(g4p, 15)]
                            v_vmm_grp[g4p] = p.val("pe")
                            if g4p + 1 < 4:
                                state["v_uvl"][g4p + 1] = emit_uv_load(g4p + 1)
                        v_kbound = p.chain("ve", [
                            lambda e, K_ch=K_ch, st_b=st_b: e.reduce_max(
                                st_b[:, 0:1], K_ch[:, :], axis=AX.X,
                                apply_absolute_value=True),
                            lambda e, st_b=st_b: e.tensor_tensor(
                                st_b[:, 0:1], st_b[:, 0:1], st_b[:, 0:1], ALU.mult),
                        ], wait=(("ac", v_kb_ev[(hp, 3)]),
                                 ("pe", v_bn_pe[hp - 2])))
                        mu_q1 = mu_q1_2[par]
                        mu_q2 = mu_q2_2[par]
                        v_qbound_ve[hp] = p.chain("ve", [
                            lambda e, q_c=q_c, st_b=st_b: e.reduce_max(
                                st_b[:, 2:3], q_c[:, :], axis=AX.X,
                                apply_absolute_value=True),
                            lambda e, st_b=st_b: e.tensor_tensor(
                                st_b[:, 2:3], st_b[:, 2:3], st_b[:, 2:3],
                                ALU.mult),
                            lambda e, mu_q1=mu_q1, st_b=st_b: e.reduce_max(
                                st_b[0:32, 3:4], mu_q1[:, :], axis=AX.X,
                                apply_absolute_value=True),
                            lambda e, st_b=st_b: e.scalar_tensor_tensor(
                                st_b[0:32, 3:4], st_b[0:32, 3:4], 2.0,
                                st_b[0:32, 3:4], ALU.mult, ALU.mult),
                            lambda e, mu_q2=mu_q2, st_b=st_b: e.reduce_max(
                                st_b[0:32, 4:5], mu_q2[:, :], axis=AX.X,
                                apply_absolute_value=True),
                            lambda e, st_b=st_b: e.scalar_tensor_tensor(
                                st_b[0:32, 4:5], st_b[0:32, 4:5], 2.0,
                                st_b[0:32, 4:5], ALU.mult, ALU.mult),
                        ], wait=(("ac", v_qcev[hp]), ("ve", v_kbound),
                                 ("ac", state["v_spq"][hp]),
                                 ("pe", v_bn_pe[hp - 2])))
                        bn_pending[hp] = v_qbound_ve[hp]

                    def inject_bn(hp):
                        par = hp % 2
                        st_b = st_b2[par]
                        v_ve = bn_pending.pop(hp)
                        def _bn(e, st_b=st_b):
                            e.matmul(psN[0:1, 0:1], c_onescf[:, :], st_b[:, 0:1],
                                     start=True, stop=False)
                            e.matmul(psN[0:1, 0:1], c_onescf[0:64, :],
                                     krot_b[:, 0:1], start=False, stop=True)
                            e.matmul(psN[0:1, 1:2], c_onescf[:, :], st_b[:, 2:3],
                                     start=True, stop=False)
                            e.matmul(psN[0:1, 1:2], c_onescf[0:32, :],
                                     st_b[0:32, 3:4], start=False, stop=False)
                            return e.matmul(psN[0:1, 1:2], c_onescf[0:32, :],
                                            st_b[0:32, 4:5], start=False, stop=True)
                        p.emit("pe", _bn,
                               wait=(("ve", v_ve),
                                     ("ve", v_bnev_last[0])),
                               inc=(("pe", 1),))
                        v_bnm = p.val("pe")
                        v_nev = p.chain("ve", [
                            lambda e: e.tensor_copy(sb_n[:, 0:2], psN),
                            lambda e: e.tensor_tensor(sb_n[:, 2:3], sb_n[:, 0:1],
                                                      sb_n[:, 1:2], ALU.mult),
                        ], wait=(("pe", v_bnm),))
                        v_bex = p.chain("ac", [
                            lambda e: e.activation(sb_n[:, 3:4], sb_n[:, 2:3],
                                                   AF.Ln),
                            lambda e: e.activation(sb_n[:, 2:3], sb_n[:, 3:4],
                                                   AF.Exp, scale=0.5),
                        ], wait=(("ve", v_nev),))
                        v_nb = p.chain("ve", [
                            lambda e: e.tensor_scalar(sb_n[:, 3:4], sb_n[:, 2:3],
                                                      -SCALE, None, ALU.mult),
                        ], wait=(("ac", v_bex),))
                        p.emit("pe", lambda e: e.matmul(
                            psBB, c_onesf[:, :], sb_n[:, 3:4],
                            start=True, stop=True),
                            wait=(("ve", v_nb),), inc=(("pe", 1),))
                        v_bbm = p.val("pe")
                        v_bn_pe[hp] = v_bbm
                        p.emit("ve", lambda e, par=par: e.tensor_copy(
                            negb_bc2[par][:, :], psBB),
                            wait=(("pe", v_bbm), ("ac", v_acend[hp - 2])),
                            inc=(("ve", 1),))
                        v_negb[hp] = p.val("ve")
                        v_bnev_last[0] = v_negb[hp]

                    state["v_uvl"] = {}
                    state["v_spq"] = {}
                    v_uq[0] = emit_uq_load(0)
                    v_uq[1] = emit_uq_load(1)
                    state["v_uvl"][0] = emit_uv_load(0)
                    emit_prelude(0)
                    emit_prelude_qr(0)
                    emit_prelude_kv(0)
                    inject_bn(0)
                    for h in range(NH):
                        g4, half, hh = h // 4, h // 8, h % 8
                        par = h % 2
                        K_ch = K_ch2[par]
                        q_c = q_c2[par]
                        q_rot = q_rot2[par]
                        V_r = V_r2[g4 % 2]

                        def emit_tail(j, slots, v_msk, hh, h, V_r):
                            eT = exp_T[j % 2]
                            def _sum(e, slots=slots, eT=eT):
                                for i, s in enumerate(slots):
                                    inst = e.matmul(
                                        psSum, c_onesb[:, :],
                                        eT[:, s * 128:(s + 1) * 128],
                                        start=(i == 0), stop=(i == len(slots) - 1))
                                return inst
                            p.emit("pe", _sum,
                                   wait=(("ve", v_msk), ("ve", state["v_sumlast"])),
                                   inc=(("pe", 1),))
                            v_summ = p.val("pe")
                            def _att(e, slots=slots, h=h, eT=eT, V_r=V_r):
                                for i, s in enumerate(slots):
                                    inst = e.matmul(
                                        psAtt,
                                        V_r[:, s, (h % 4) * 128:(h % 4 + 1) * 128],
                                        eT[:, s * 128:(s + 1) * 128],
                                        start=(i == 0), stop=(i == len(slots) - 1))
                                return inst
                            p.emit("pe", _att, wait=(("ve", state["v_attev"]),),
                                   inc=(("pe", 1),))
                            v_attm = p.val("pe")
                            state["v_attm"][j] = v_attm
                            p.emit("ve", lambda e: e.reciprocal(
                                sc_rcp[:, :], psSum),
                                wait=(("pe", v_summ), ("ve", state["v_attev"])),
                                inc=(("ve", 1),))
                            state["v_sumlast"] = p.val("ve")
                            p.emit("ve", lambda e, hh=hh, j=j: e.tensor_tensor(
                                att_all[:, (hh * 8 + j) * 128:(hh * 8 + j + 1) * 128],
                                psAtt, sc_rcp[:, :], ALU.mult),
                                wait=(("ve", state["v_sumlast"]), ("pe", v_attm)),
                                inc=(("ve", 1),))
                            state["v_attev"] = p.val("ve")

                        state.setdefault("v_attm", {})
                        state.setdefault("v_sumlast", 0)
                        state.setdefault("v_attev", 0)
                        state.setdefault("v_msk", {})
                        pend = state.get("pend")
                        for j in range(TQ):
                            eT = exp_T[j % 2]
                            slots = [kt for kt in range(j + 1)] +                                     [8 + o for o in range(j + 1)]
                            groups = [slots[i:i + 4]
                                      for i in range(0, len(slots), 4)]
                            v_sc = {}
                            for gi, grp in enumerate(groups):
                                pb = gi % 2
                                def _smm(e, grp=grp, pb=pb, j=j, K_ch=K_ch,
                                         q_c=q_c, q_rot=q_rot):
                                    for i, s in enumerate(grp):
                                        e.matmul(
                                            psS[pb][:, i * 128:(i + 1) * 128],
                                            K_ch[:, s * 128:(s + 1) * 128],
                                            q_c[:, j * 128:(j + 1) * 128],
                                            start=True, stop=False)
                                        inst = e.matmul(
                                            psS[pb][:, i * 128:(i + 1) * 128],
                                            krot[:, s * 128:(s + 1) * 128],
                                            q_rot[:, j * 128:(j + 1) * 128],
                                            start=False, stop=True)
                                    return inst
                                wmm = [("po", v_rope[h]),
                                       ("ac", v_qcev[h])]
                                if gi >= 2:
                                    wmm.append(("ac", v_sc[gi - 2]))
                                else:
                                    wmm.append(("ac", v_expS.get(pb, 0)))
                                    wmm.append(("ve", state["v_phc_last"]))
                                p.emit("pe", _smm, wait=tuple(wmm),
                                       inc=(("pe", 1),))
                                v_m = p.val("pe")
                                segs = []
                                run = []
                                for i, s in enumerate(grp):
                                    if not (s == j or s == 8 + j):
                                        run.append((i, s))
                                    else:
                                        if run:
                                            segs.append(run)
                                        run = []
                                if run:
                                    segs.append(run)
                                def _exp(e, segs=segs, grp=grp, pb=pb, j=j,
                                         eT=eT, par=par):
                                    inst = None
                                    nb = negb_bc2[par]
                                    for seg in segs:
                                        i0, s0 = seg[0]
                                        n = len(seg)
                                        inst = e.activation(
                                            eT[:, s0 * 128:(s0 + n) * 128],
                                            psS[pb][:, i0 * 128:(i0 + n) * 128],
                                            AF.Exp, scale=SCALE, bias=nb[:, :])
                                    for i, s in enumerate(grp):
                                        if s == j:
                                            inst = e.activation(
                                                sc_e_2[j % 2][:, :],
                                                psS[pb][:, i * 128:(i + 1) * 128],
                                                AF.Exp, scale=SCALE, bias=nb[:, :])
                                        elif s == 8 + j:
                                            inst = e.activation(
                                                sc_e2_2[j % 2][:, :],
                                                psS[pb][:, i * 128:(i + 1) * 128],
                                                AF.Exp, scale=SCALE, bias=nb[:, :])
                                    return inst
                                wexp = [("pe", v_m), ("ve", v_negb[h]),
                                        ("pe", state["v_attm"].get(
                                            j - 2 if j >= 2 else 6 + j, 0)),
                                        ("ve", state["v_msk"].get(
                                            (j - 2) % TQ, 0))]
                                p.emit("ac", _exp, wait=tuple(wexp),
                                       inc=(("ac", 1),))
                                v_sc[gi] = p.val("ac")
                                v_expS[pb] = v_sc[gi]
                            v_msk = p.chain("ve", [
                                lambda e, j=j, eT=eT: e.tensor_tensor(
                                    eT[:, j * 128:(j + 1) * 128],
                                    sc_e_2[j % 2][:, :],
                                    c_tri[:, :], ALU.mult),
                                lambda e, j=j, eT=eT: e.tensor_scalar(
                                    eT[:, (8 + j) * 128:(9 + j) * 128],
                                    sc_e2_2[j % 2][:, :], c_padk[:, j:j + 1],
                                    None, ALU.mult),
                            ], wait=(("ac", v_sc[len(groups) - 1]),
                                     ("spk", v_padk)))
                            state["v_msk"][j] = v_msk
                            newpend = (j, slots, v_msk, hh, h, V_r)
                            if pend is not None:
                                emit_tail(*pend)
                            pend = newpend
                            if j == 0 and h + 1 < NH:
                                emit_prelude(h + 1)
                            if j == 1 and h + 1 < NH:
                                emit_prelude_qr(h + 1)
                            if j == 4 and h + 1 < NH:
                                emit_prelude_kv(h + 1)
                            if j == 6 and (h + 1) in bn_pending:
                                inject_bn(h + 1)
                        state["pend"] = pend
                        v_acend[h] = p.val("ac")

                        if h == 4:
                            p.emit("pl", lambda e: e.dma_start(
                                wout_sb[:, :], wout[:, 0:8 * DM]),
                                inc=(("pwo0", 16),))
                            v_wout[0] = ("pwo0", p.val("pwo0"))
                        if h == 12:
                            p.emit("pl", lambda e: e.dma_start(
                                wout_sb[:, :], wout[:, 8 * DM:16 * DM]),
                                wait=(("pe", v_phc_mm),), inc=(("pwo1", 16),))
                            v_wout[1] = ("pwo1", p.val("pwo1"))
                        if h in (7, 15):
                            flush_pend()
                            k = h // 8
                            v_phc_mm, v_phc = emit_phc(k, v_wout[k],
                                                       state["v_attev"])
                            state["v_phc_last"] = v_phc
                            p.drain_dma()

        es_ffn = contextlib.ExitStack()
        with es_ffn:
            def sbf(name, shape, dt=F32):
                return es_ffn.enter_context(nc.sbuf_tensor(name, shape, dt))
            x_res = sbf("x_res", [128, TQ * DM])
            xln2 = sbf("xln2", [128, NCH * 1024], BF16)
            xln_t2 = [sbf(f"xlnt2{i}", [128, DM], BF16) for i in range(2)]
            st2 = sbf("st2", [128, 16])
            st2s = sbf("st2s", [128, 4])
            h_sb = sbf("h_sb", [128, NFCH * 1024], BF16)
            wd_sb = [sbf(f"wd_sb{i}", [128, NFCH * 512], BF16) for i in range(2)]
            wg_sl = [sbf(f"wg_sl{i}", [128, DM], BF16) for i in range(4)]
            wu_sl = [sbf(f"wu_sl{i}", [128, DM], BF16) for i in range(4)]
            C_bc = sbf("C_bc", [128, 1024])
            CT = sbf("CT", [8, 1024])
            C_tok = sbf("C_tok", [128, 8])
            lgt_sb = sbf("lgt_sb", [128, 8])
            top8 = sbf("top8", [128, 8])
            p8 = sbf("p8", [128, 8])
            mask8 = sbf("mask8", [128, 8])
            t8 = sbf("t8", [128, 8])
            lsum = sbf("lsum", [128, 2])

            sig_row = sbf("sig_row", [1, 1024])
            t1_sb = [sbf(f"t1_sb{i}", [128, 1024], BF16) for i in range(2)]
            silu_g = [sbf(f"silu_g{i}", [128, 1024], BF16) for i in range(2)]
            wr_sb = sbf("wr_sb", [128, NCH * EXP], BF16)
            wsg_sb = sbf("wsg_sb", [128, NCH], BF16)
            xres_r = x_res[:, :].rearrange("p (t s) -> p t s", t=TQ)
            xln2_r = xln2[:, :].rearrange("p (c s) -> p c s", c=NCH)
            h_r = h_sb[:, :].rearrange("p (f s) -> p f s", f=NFCH)

            es_psD = contextlib.ExitStack()
            with es_psD:
                junk2 = es_psD.enter_context(
                    nc.sbuf_tensor("junk2", [128, DM], BF16))
                identf_e = es_psD.enter_context(
                    nc.sbuf_tensor("identf_e", [128, 129], F32))
                psT2 = [es_psD.enter_context(
                    nc.psum_tensor(f"psT2{i}", [128, 1024], BF16)) for i in range(2)]
                psL = es_psD.enter_context(nc.psum_tensor("psL", [128, 8], F32))
                psCT = es_psD.enter_context(nc.psum_tensor("psCT", [8, 128], F32))
                psSG = es_psD.enter_context(nc.psum_tensor("psSG", [1, 1024], F32))
                v_xld = {}
                v_n2 = {}
                v_tr2 = {}
                v_ev2 = {}
                v_mv2 = {}
                for j in range(TQ):
                    sem = f"sxr{j % 2}"
                    w = ((sem, p.val(sem)),) if j >= 2 else ()
                    p.emit("sp", lambda e, j=j: e.dma_start(
                        xres_r[:, j, :], xprime[j]), wait=w, inc=((sem, 16),))
                    v_xld[j] = (sem, p.val(sem))
                for j in range(TQ):
                    bi = j % 2
                    o = 8 * bi
                    wst = [v_xld[j]]
                    if j >= 2:
                        wst.append(("ve", v_mv2[j - 2]))
                    p.emit("ve", lambda e, j=j, bi=bi: e.reduce_sum(
                        st2s[:, bi:bi + 1], xres_r[:, j, :], axis=AX.X),
                        wait=tuple(wst), inc=(("ve", 1),))
                    v_rs = p.val("ve")
                    p.emit("ac", lambda e, j=j, bi=bi: e.activation(
                        junk2[:, :], xres_r[:, j, :], AF.Square,
                        accum_out=st2s[:, 2 + bi:3 + bi]),
                        wait=(v_xld[j],), inc=(("ac", 1),))
                    v_sq = p.val("ac")
                    wmv = [("ac", v_sq), ("ve", v_rs)]
                    if j >= 2:
                        wmv.append(("ve", v_n2[j - 2]))
                    v_mv2[j] = p.chain("ve", [
                        lambda e, bi=bi, o=o: e.tensor_scalar(
                            st2[:, o:o + 1], st2s[:, bi:bi + 1],
                            1.0 / DM, None, ALU.mult),
                        lambda e, bi=bi, o=o: e.tensor_scalar(
                            st2[:, o + 1:o + 2], st2s[:, 2 + bi:3 + bi],
                            1.0 / DM, EPS, ALU.mult, ALU.add),
                        lambda e, o=o: e.tensor_tensor(
                            st2[:, o + 2:o + 3], st2[:, o:o + 1],
                            st2[:, o:o + 1], ALU.mult),
                        lambda e, o=o: e.tensor_tensor(
                            st2[:, o + 1:o + 2], st2[:, o + 1:o + 2],
                            st2[:, o + 2:o + 3], ALU.subtract),
                    ], wait=tuple(wmv))
                    v_r2 = p.chain("ac", [
                        lambda e, o=o: e.activation(
                            st2[:, o + 2:o + 3], st2[:, o + 1:o + 2], AF.Ln),
                        lambda e, o=o: e.activation(
                            st2[:, o + 3:o + 4], st2[:, o + 2:o + 3],
                            AF.Exp, scale=-0.5),
                    ], wait=(("ve", v_mv2[j]),))
                    wn = [("ac", v_r2)]
                    if j >= 2:
                        wn.append(("pe", v_tr2[j - 2]))
                    v_n2[j] = p.chain("ve", [
                        lambda e, o=o: e.scalar_tensor_tensor(
                            st2[:, o:o + 1], st2[:, o:o + 1], -1.0,
                            st2[:, o + 3:o + 4], ALU.mult, ALU.mult),
                        lambda e, j=j, bi=bi, o=o: e.tensor_scalar(
                            xln_t2[bi][:, :], xres_r[:, j, :],
                            st2[:, o + 3:o + 4], st2[:, o:o + 1],
                            ALU.mult, ALU.add),
                    ], wait=tuple(wn))
                    for hb in range(2):
                        wtr = [("ve", v_n2[j])]
                        if j >= 1:
                            wtr.append(("ve", v_ev2.get((j - 1, hb), 0)))
                        def _tr(e, bi=bi, hb=hb):
                            for c8 in range(8):
                                c = hb * 8 + c8
                                inst = e.transpose(
                                    psT2[hb][:, c8 * 128:(c8 + 1) * 128],
                                    xln_t2[bi][:, c * 128:(c + 1) * 128],
                                    c_identb[:, :])
                            return inst
                        p.emit("pe", _tr, wait=tuple(wtr), inc=(("pe", 1),))
                        v_tr2[j] = p.val("pe")
                        def _ev(e, hb=hb, j=j):
                            dst = xln2_r[:, hb * 8:(hb + 1) * 8,
                                         j * 128:(j + 1) * 128]
                            return e.tensor_copy(
                                dst, psT2[hb][:, :].rearrange(
                                    "p (c q) -> p c q", c=8))
                        p.emit("ve", _ev, wait=(("pe", v_tr2[j]),),
                               inc=(("ve", 1),))
                        v_ev2[(j, hb)] = p.val("ve")
                v_xln2 = p.val("ve")
                p.flush()
                do_tap("xln2", xln2[:, :])

                p.emit("pl", lambda e: e.iota(
                    identf_e[:, 0:128], [[1, 128]], channel_multiplier=0,
                    allow_small_or_imprecise_dtypes=True), inc=(("po", 1),))
                p.emit("pl", lambda e: e.iota(
                    identf_e[:, 128:129], [[0, 1]], channel_multiplier=1,
                    allow_small_or_imprecise_dtypes=True), inc=(("po", 1),))
                v_iot2 = p.val("po")
                p.emit("ve", lambda e: e.tensor_scalar(
                    identf_e[:, 0:128], identf_e[:, 0:128],
                    identf_e[:, 128:129], None, ALU.is_equal),
                    wait=(("po", v_iot2),), inc=(("ve", 1),))
                v_identf = p.val("ve")
                p.emit("pl", lambda e: e.dma_start(wr_sb[:, :], wr_i[:, :]),
                       inc=(("prt", 16),))
                p.emit("pl", lambda e: e.dma_start(wsg_sb[:, :], wsg_i[:, :]),
                       inc=(("prt", 16),))
                v_prt = p.val("prt")
                v_ctev = 0
                v_lcp = 0
                v_pex_prev = 0
                v_ct_prev = 0
                v_ctm_prev = 0
                for j in range(TQ):
                    def _lmm(e, j=j):
                        for c in range(NCH):
                            inst = e.matmul(
                                psL[:, :], xln2_r[:, c, j * 128:(j + 1) * 128],
                                wr_sb[:, c * EXP:(c + 1) * EXP],
                                start=(c == 0), stop=(c == NCH - 1))
                        return inst
                    p.emit("pe", _lmm,
                           wait=(("prt", v_prt), ("ve", v_xln2), ("ve", v_lcp)),
                           inc=(("pe", 1),))
                    v_lm = p.val("pe")
                    v_lcp = p.chain("ve", [
                        lambda e: e.tensor_copy(lgt_sb[:, :], psL[:, :]),
                        lambda e: e.max(top8[:, :], lgt_sb[:, :]),
                        lambda e: e.tensor_scalar(lsum[:, 1:2], top8[:, 0:1],
                                                  -1.0, None, ALU.mult),
                    ], wait=(("pe", v_lm), ("ac", v_pex_prev),
                             ("ve", v_ct_prev)))
                    p.emit("ac", lambda e: e.activation(
                        p8[:, :], lgt_sb[:, :], AF.Exp, bias=lsum[:, 1:2],
                        accum_out=lsum[:, 0:1]),
                        wait=(("ve", v_lcp), ("ve", v_ct_prev)),
                        inc=(("ac", 1),))
                    v_pex = p.val("ac")
                    v_pex_prev = v_pex
                    v_ct = p.chain("ve", [
                        lambda e: e.reciprocal(lsum[:, 1:2], lsum[:, 0:1]),
                        lambda e: e.tensor_scalar(mask8[:, :], lgt_sb[:, :],
                                                  top8[:, 1:2], None, ALU.is_ge),
                        lambda e: e.tensor_scalar(t8[:, :], p8[:, :],
                                                  lsum[:, 1:2], None, ALU.mult),
                        lambda e: e.tensor_tensor(C_tok[:, :], t8[:, :],
                                                  mask8[:, :], ALU.mult),
                    ], wait=(("ac", v_pex), ("pe", v_ctm_prev)))
                    v_ct_prev = v_ct
                    p.emit("pe", lambda e: e.transpose(
                        psCT[:, :], C_tok[:, :], identf_e[:, 0:128]),
                        wait=(("ve", v_ct), ("ve", v_ctev), ("ve", v_identf)),
                        inc=(("pe", 1),))
                    v_ctm = p.val("pe")
                    v_ctm_prev = v_ctm
                    p.emit("ve", lambda e, j=j: e.tensor_copy(
                        CT[:, j * 128:(j + 1) * 128], psCT[:, :]),
                        wait=(("pe", v_ctm),), inc=(("ve", 1),))
                    v_ctev = p.val("ve")
                for tb in range(2):
                    def _sgm(e, tb=tb):
                        for c in range(NCH):
                            inst = e.matmul(
                                psSG[0:1, tb * 512:(tb + 1) * 512],
                                wsg_sb[:, c:c + 1],
                                xln2_r[:, c, tb * 512:(tb + 1) * 512],
                                start=(c == 0), stop=(c == NCH - 1))
                        return inst
                    p.emit("pe", _sgm, wait=(("prt", v_prt), ("ve", v_xln2)),
                           inc=(("pe", 1),))
                v_sgm = p.val("pe")
                p.emit("ac", lambda e: e.activation(
                    sig_row[:, :], psSG[0:1, :], AF.Sigmoid),
                    wait=(("pe", v_sgm),), inc=(("ac", 1),))
                v_sig = p.val("ac")
                p.flush()
                do_tap("CT", CT[:, :])
                do_tap("sigrow", sig_row[:, :])

            rowbuf = sbf("rowbuf", [1, 1024])
            es_psF = contextlib.ExitStack()
            with es_psF:
                psGU = [es_psF.enter_context(
                    nc.psum_tensor(f"psGU{i}", [128, 1024], F32)) for i in range(4)]
                psG = [psGU[0], psGU[1]]
                psU = [psGU[2], psGU[3]]
                psDr = [t[:, hb * 512:(hb + 1) * 512]
                        for t in psGU for hb in range(2)]

                def slab_g(pidx, f):
                    return wg_i[pidx, f] if pidx < 8 else \
                        sg_i[(pidx - 8) * NFCH + f]

                def slab_u(pidx, f):
                    return wu_i[pidx, f] if pidx < 8 else \
                        su_i[(pidx - 8) * NFCH + f]

                def slab_d(pidx, oq):
                    if pidx < 8:
                        return wd_r[pidx][:, :, oq * 512:(oq + 1) * 512]
                    g = pidx - 8
                    return sd_r[:, g * NFCH:(g + 1) * NFCH,
                                oq * 512:(oq + 1) * 512]

                NLIN = 12 * NFCH
                v_gu = {}
                v_wd = {}
                v_gmm = {}
                v_umm = {}
                v_silu = {}
                v_h = {-1: 0}
                v_dmm_oq = {}
                v_evD2 = {-1: 0}
                v_cbmm = 0

                def emit_slabs(n):
                    pidx, f = n // NFCH, n % NFCH
                    sl = n % 4
                    p.emit("pl", lambda e, pidx=pidx, f=f, sl=sl: e.dma_start(
                        wg_sl[sl][:, :], slab_g(pidx, f)),
                        wait=(("pe", v_gmm.get(n - 4, 0)),), inc=((f"pg{sl}", 16),))
                    vg = (f"pg{sl}", p.val(f"pg{sl}"))
                    p.emit("pl", lambda e, pidx=pidx, f=f, sl=sl: e.dma_start(
                        wu_sl[sl][:, :], slab_u(pidx, f)),
                        wait=(("pe", v_umm.get(n - 4, 0)),), inc=((f"pu{sl}", 16),))
                    vu = (f"pu{sl}", p.val(f"pu{sl}"))
                    v_gu[n] = (vg, vu)

                def emit_wd(pidx, oq):
                    nd = pidx * 4 + oq
                    sl = nd % 2
                    w = v_dmm_oq.get((pidx, oq - 2),
                                     v_dmm_oq.get((pidx - 1, 2 + oq), 0))
                    p.emit("pl", lambda e, pidx=pidx, oq=oq, sl=sl: e.dma_start(
                        wd_sb[sl][:, :].rearrange("p (f s) -> p f s", f=NFCH),
                        slab_d(pidx, oq)),
                        wait=(("pe", w),), inc=((f"pd{sl}", 16),))
                    v_wd[(pidx, oq)] = (f"pd{sl}", p.val(f"pd{sl}"))

                for n0 in range(4):
                    emit_slabs(n0)
                for pidx in range(12):
                    if pidx < 8:
                        sl = pidx % 2
                        wrow = [("pe", v_cbmm)] if pidx >= 2 else                             [("ve", v_ctev)]
                        p.emit("sp", lambda e, pidx=pidx, sl=sl: e.dma_start(
                            rowbuf[:, :], CT[pidx:pidx + 1, :]),
                            wait=tuple(wrow), inc=((f"scr{sl}", 16),))
                        v_row = (f"scr{sl}", p.val(f"scr{sl}"))
                        rowsrc = rowbuf
                    else:
                        v_row = ("ac", v_sig)
                        rowsrc = sig_row
                    def _cbm(e, rowsrc=rowsrc):
                        e.matmul(psDr[0][:, :], c_onesf[:, :], rowsrc[0:1, 0:512],
                                 start=True, stop=True)
                        return e.matmul(psDr[1][:, :], c_onesf[:, :],
                                        rowsrc[0:1, 512:1024],
                                        start=True, stop=True)
                    p.emit("pe", _cbm,
                           wait=(v_row, ("ve", v_evD2.get((pidx - 1, 25), 0))),
                           inc=(("pe", 1),))
                    v_cbmm = p.val("pe")
                    v_cbev = p.chain("ve", [
                        lambda e: e.tensor_copy(C_bc[:, 0:512], psDr[0][:, :]),
                        lambda e: e.tensor_copy(C_bc[:, 512:1024], psDr[1][:, :]),
                    ], wait=(("pe", v_cbmm), ("ve", v_h[pidx * NFCH - 1])))
                    for f in range(NFCH):
                        n = pidx * NFCH + f
                        sl = n % 4
                        pp = n % 2
                        vg, vu = v_gu[n]
                        def _gmm(e, sl=sl, pp=pp):
                            for tb in range(2):
                                for c in range(NCH):
                                    inst = e.matmul(
                                        psG[pp][:, tb * 512:(tb + 1) * 512],
                                        wg_sl[sl][:, c * 128:(c + 1) * 128],
                                        xln2_r[:, c, tb * 512:(tb + 1) * 512],
                                        start=(c == 0), stop=(c == NCH - 1))
                            return inst
                        wg_mm = [vg, ("ac", v_silu.get(n - 2, 0)), ("ve", v_xln2)]
                        if f <= 1:
                            wg_mm.append(
                                ("ve", v_evD2.get((pidx - 1, 25 + 2 * pp), 0)))
                            if pp == 0:
                                wg_mm.append(("ve", v_cbev))
                        p.emit("pe", _gmm, wait=tuple(wg_mm), inc=(("pe", 1),))
                        v_gmm[n] = p.val("pe")
                        def _umm(e, sl=sl, pp=pp):
                            for tb in range(2):
                                for c in range(NCH):
                                    inst = e.matmul(
                                        psU[pp][:, tb * 512:(tb + 1) * 512],
                                        wu_sl[sl][:, c * 128:(c + 1) * 128],
                                        xln2_r[:, c, tb * 512:(tb + 1) * 512],
                                        start=(c == 0), stop=(c == NCH - 1))
                            return inst
                        wu_mm = [vu, ("ve", v_h.get(n - 2, 0))]
                        if f <= 1:
                            wu_mm.append(
                                ("ve", v_evD2.get((pidx - 1, 29 + 2 * pp), 0)))
                        p.emit("pe", _umm, wait=tuple(wu_mm), inc=(("pe", 1),))
                        v_umm[n] = p.val("pe")
                        p.emit("ac", lambda e, pp=pp: e.activation(
                            silu_g[pp][:, :], psG[pp][:, :], AF.Silu),
                            wait=(("pe", v_gmm[n]), ("ve", v_h.get(n - 1, 0))),
                            inc=(("ac", 1),))
                        v_silu[n] = p.val("ac")
                        wh = [("ac", v_silu[n]), ("ve", v_cbev),
                              ("pe", v_umm[n]), ("ve", v_h.get(n - 2, 0))]
                        v_h[n] = p.chain("ve", [
                            lambda e, pp=pp: e.tensor_tensor(
                                t1_sb[pp][:, :], silu_g[pp][:, :], C_bc[:, :],
                                ALU.mult),
                            lambda e, f=f, pp=pp: e.tensor_tensor(
                                h_r[:, f, :], t1_sb[pp][:, :], psU[pp][:, :],
                                ALU.mult),
                        ], wait=tuple(wh))
                        if n + 4 < NLIN:
                            emit_slabs(n + 4)
                        if f == 0:
                            emit_wd(pidx, 0)
                        elif f == 1:
                            emit_wd(pidx, 1)
                    v_hlast = v_h[pidx * NFCH + NFCH - 1]
                    n8 = pidx * NFCH + 8
                    n9 = pidx * NFCH + 9
                    n10 = pidx * NFCH + 10
                    pp9, pp10 = n9 % 2, n10 % 2
                    bankfree = {pp9: ("ac", v_silu[n9]),
                                pp10: ("ac", v_silu[n10]),
                                2 + pp9: ("ve", v_h[n9]),
                                2 + pp10: ("ve", v_h[n10])}
                    jorder = [2 * pp9, 2 * pp9 + 1,
                              2 * (2 + pp9), 2 * (2 + pp9) + 1,
                              2 * pp10, 2 * pp10 + 1,
                              2 * (2 + pp10), 2 * (2 + pp10) + 1]
                    sl0 = (pidx * 4) % 2
                    vB = {}
                    for j in jorder:
                        def _dmmA(e, j=j, sl=sl0):
                            for f in range(9):
                                inst = e.matmul(
                                    psDr[j][:, :],
                                    h_r[:, f, j * 128:(j + 1) * 128],
                                    wd_sb[sl][:, f * 512:(f + 1) * 512],
                                    start=(f == 0), stop=False)
                            return inst
                        p.emit("pe", _dmmA,
                               wait=(v_wd[(pidx, 0)], ("ve", v_h[n8]),
                                     bankfree[j // 2]),
                               inc=(("pe", 1),))
                    for j in jorder:
                        def _dmmB(e, j=j, sl=sl0):
                            e.matmul(psDr[j][:, :],
                                     h_r[:, 9, j * 128:(j + 1) * 128],
                                     wd_sb[sl][:, 9 * 512:10 * 512],
                                     start=False, stop=False)
                            return e.matmul(psDr[j][:, :],
                                            h_r[:, 10, j * 128:(j + 1) * 128],
                                            wd_sb[sl][:, 10 * 512:11 * 512],
                                            start=False, stop=True)
                        p.emit("pe", _dmmB,
                               wait=(v_wd[(pidx, 0)], ("ve", v_hlast)),
                               inc=(("pe", 1),))
                        vB[j] = p.val("pe")
                    v_dmm_oq[(pidx, 0)] = p.val("pe")
                    for j in jorder:
                        p.emit("ve", lambda e, j=j: e.tensor_tensor(
                            xres_r[:, j, 0:512], psDr[j][:, :],
                            xres_r[:, j, 0:512], ALU.add),
                            wait=(("pe", vB[j]),
                                  ("ve", v_evD2.get((pidx - 1, j), 0))),
                            inc=(("ve", 1),))
                        v_evD2[(pidx, j)] = p.val("ve")
                    emit_wd(pidx, 2)
                    for oq in range(1, 4):
                        sl = (pidx * 4 + oq) % 2
                        for j in range(TQ):
                            lin = oq * TQ + j
                            pb = lin % 8
                            def _dmm(e, oq=oq, j=j, pb=pb, sl=sl):
                                for f in range(NFCH):
                                    inst = e.matmul(
                                        psDr[pb][:, :],
                                        h_r[:, f, j * 128:(j + 1) * 128],
                                        wd_sb[sl][:, f * 512:(f + 1) * 512],
                                        start=(f == 0), stop=(f == NFCH - 1))
                                return inst
                            wdm = [v_wd[(pidx, oq)], ("ve", v_hlast),
                                   ("ve", v_evD2.get((pidx, lin - 8), v_cbev))]
                            p.emit("pe", _dmm, wait=tuple(wdm), inc=(("pe", 1),))
                            v_dm = p.val("pe")
                            v_dmm_oq[(pidx, oq)] = v_dm
                            wev = [("pe", v_dm),
                                   ("ve", v_evD2.get((pidx - 1, lin), 0))]
                            p.emit("ve", lambda e, oq=oq, j=j, pb=pb:
                                   e.tensor_tensor(
                                       xres_r[:, j, oq * 512:(oq + 1) * 512],
                                       psDr[pb][:, :],
                                       xres_r[:, j, oq * 512:(oq + 1) * 512],
                                       ALU.add),
                                   wait=tuple(wev), inc=(("ve", 1),))
                            v_evD2[(pidx, lin)] = p.val("ve")
                        if oq + 2 < 4:
                            emit_wd(pidx, oq + 2)
                    v_evD2[pidx] = p.val("ve")
                    p.flush()
                for j in range(TQ):
                    p.emit("sp", lambda e, j=j: e.dma_start(
                        y_out[j], xres_r[:, j, :]),
                        wait=(("ve", v_evD2[11]),), inc=((f"sy{j}", 16),))
                p.flush()

        p.final_waits()
        p.flush()

    nc.finalize()
    return nc, taps


def emit_ffn(nc, p, es0, sb, env):
    pass


def _host_inputs(inp):
    f32 = np.float32
    x = np.asarray(inp["L0_attn_x"], f32)
    base = {}
    base["wdkv"] = _pk(np.asarray(inp["L0_attn_w_dkv"], f32))
    base["wdq"] = _pk(np.asarray(inp["L0_attn_w_dq"], f32))
    base["wkr"] = _pk(np.asarray(inp["L0_attn_w_kr"], f32))
    base["wuk"] = _pk(np.asarray(inp["L0_attn_w_uk"], f32))
    base["wuv"] = _pk(np.asarray(inp["L0_attn_w_uv"], f32))
    base["wuq"] = _pk(np.asarray(inp["L0_attn_w_uq"], f32))
    base["wqr"] = _pk(np.asarray(inp["L0_attn_w_qr"], f32))
    base["wout"] = _pk(np.asarray(inp["L0_attn_w_out"], f32))
    base["wr_i"] = _pk(np.asarray(inp["L0_ffn_Wr"], f32))
    base["wsg_i"] = _pk(np.asarray(inp["L0_ffn_Wsg"], f32))
    wg = np.asarray(inp["L0_ffn_Wg"], f32)
    wu = np.asarray(inp["L0_ffn_Wu"], f32)
    base["wg_i"] = np.ascontiguousarray(
        wg.reshape(EXP, NCH, 128, NFCH, 128).transpose(0, 3, 2, 1, 4)
        .reshape(EXP, NFCH, 128, DM))
    base["wu_i"] = np.ascontiguousarray(
        wu.reshape(EXP, NCH, 128, NFCH, 128).transpose(0, 3, 2, 1, 4)
        .reshape(EXP, NFCH, 128, DM))
    sg = np.asarray(inp["L0_ffn_Sg"], f32)
    su = np.asarray(inp["L0_ffn_Su"], f32)
    base["sg_i"] = np.ascontiguousarray(
        sg.reshape(NCH, 128, 4 * NFCH, 128).transpose(2, 1, 0, 3)
        .reshape(4 * NFCH, 128, DM))
    base["su_i"] = np.ascontiguousarray(
        su.reshape(NCH, 128, 4 * NFCH, 128).transpose(2, 1, 0, 3)
        .reshape(4 * NFCH, 128, DM))
    base["wd_i"] = np.ascontiguousarray(np.asarray(inp["L0_ffn_Wd"], f32))
    base["sd_i"] = np.ascontiguousarray(np.asarray(inp["L0_ffn_Sd"], f32))
    base["delta_i"] = np.asarray(inp["L0_attn_delta"], f32).reshape(32, 1)
    tri = (np.arange(128)[:, None] <= np.arange(128)[None, :]).astype(f32)
    base["triT"] = tri
    base["ones_f"] = np.ones((1, 128), dtype=f32)
    base["onescol_f"] = np.ones((128, 1), dtype=f32)

    theta = (10000.0 ** (-2.0 * np.arange(32, dtype=np.float64) / DR))
    halfdat = []
    for half in (0, 1):
        pos = np.concatenate([np.arange(t * 128, (t + 1) * 128)
                              for t in PERM[half]]).astype(np.float64)
        ang = pos[None, :] * theta[:, None]
        halfdat.append({
            "cosA": np.cos(ang).astype(f32),
            "sinA": np.sin(ang).astype(f32),
            "padk_i": np.broadcast_to(
                np.asarray(PADK[half], f32)[None, :], (128, TQ)).copy(),
        })
    in_maps = []
    for c in range(8):
        b, half = c // 2, c % 2
        m = dict(base)
        m.update(halfdat[half])
        m["x_in"] = np.ascontiguousarray(
            x[b].reshape(NT, 128, DM)[PERM[half]])
        in_maps.append(m)
    return in_maps


def kernel(**inputs):
    in_maps = _host_inputs(inputs)
    nc, taps = build_program()
    res = run_bass_kernel_spmd(nc, in_maps, core_ids=list(range(8)))
    y = np.empty((4, 2048, 2048), np.float32)
    for c in range(8):
        b, half = c // 2, c % 2
        shard = np.asarray(res.results[c]["y"], np.float32).reshape(TQ, 128, DM)
        yb = y[b].reshape(NT, 128, DM)
        for j in range(TQ):
            yb[QT[half][j]] = shard[j]
    return y


def run_debug(inputs, ncores=2):
    in_maps = _host_inputs(inputs)[:ncores]
    nc, taps = build_program()
    res = run_bass_kernel_spmd(nc, in_maps, core_ids=list(range(ncores)))
    return res
